# version 03 (final)
# baseline (speedup 1.0000x reference)
import contextlib
import numpy as np
import ml_dtypes
import concourse.bass as bass
import concourse.mybir as mybir
from concourse.bass_utils import run_bass_kernel_spmd

F32 = mybir.dt.float32
BF16 = mybir.dt.bfloat16
AF = mybir.ActivationFunctionType
ALU = mybir.AluOpType
AX = mybir.AxisListType.X

D = 2048
T = 2048
H = 32
DH = 64
BS = 256
NB = 8
NG = 8
GH = 4
DC = 16
TCH = 16
EPS = 1e-5

ENGINES = ("pe", "dve", "act", "sp", "pool")


DMA_SEMS = ("ld1", "ld2", "ld3", "ps1")


class Prog:
    def __init__(self):
        self.stream = {e: [] for e in ENGINES}
        self.count = {}
        self.last_dma = {}

    def sem_init(self, names):
        for n in names:
            self.count[n] = 0

    def add(self, eng, fn, waits=(), inc=None):
        waits = list(waits)
        if inc is not None and inc[0] in DMA_SEMS:
            key = (eng, inc[0])
            prev = self.last_dma.get(key)
            if prev is not None:
                waits.append((inc[0], prev))
        self.stream[eng].append((fn, tuple(waits), inc))
        if inc is not None:
            self.count[inc[0]] += inc[1]
            if inc[0] in DMA_SEMS:
                self.last_dma[(eng, inc[0])] = self.count[inc[0]]
            return self.count[inc[0]]
        return None

    def now(self, sem):
        return self.count[sem]


def _slope_tables():
    base = np.float32(1.0 / 2.0 ** (8.0 / H))
    rate = np.power(base, (np.arange(H, dtype=np.float32) + np.float32(1.0))).astype(np.float32)
    factor = np.float32(1.0 - 0.0 / (H - 1 + 1e-5) + 1e-5)
    slope = (rate * factor).astype(np.float32)
    r = np.arange(BS, dtype=np.float32) + np.float32(1.0)
    qd = np.exp(-(slope[:, None] * r[None, :])).astype(np.float32)
    kd = np.exp(-(slope[:, None] * (np.float32(BS) - r[None, :]))).astype(np.float32)
    ddm = r[:, None] - r[None, :]
    dd = np.where(ddm[None] >= 0, np.exp(-(slope[:, None, None] * np.abs(ddm)[None])), np.float32(0.0)).astype(np.float32)
    bd = np.exp(-(slope * np.float32(BS))).astype(np.float32)
    return slope, qd, kd, dd, bd


def make_tables():
    _, qd, kd, dd, bd = _slope_tables()
    dd_t = np.ascontiguousarray(np.transpose(dd, (0, 2, 1)))
    qd_pair = np.zeros((16, 128, BS), np.float32)
    kd_pair = np.zeros((16, 128, BS), np.float32)
    for pr in range(16):
        qd_pair[pr, 0:64, :] = qd[2 * pr][None, :]
        qd_pair[pr, 64:128, :] = qd[2 * pr + 1][None, :]
        kd_pair[pr, 0:64, :] = kd[2 * pr][None, :]
        kd_pair[pr, 64:128, :] = kd[2 * pr + 1][None, :]
    kdp_tokg = np.zeros((NG, TCH, 128, GH * DH), np.float32)
    for g in range(NG):
        for ch in range(TCH):
            n = 7 - ch // 2
            for hl in range(GH):
                h = 4 * g + hl
                kdp_tokg[g, ch, :, hl * DH:(hl + 1) * DH] = (
                    kd[h, (ch % 2) * 128:((ch % 2) + 1) * 128] * (bd[h] ** np.float32(n))
                )[:, None]
    return dd_t, qd_pair, kd_pair, kdp_tokg, bd


def build_program(debug=False):
    nc = bass.Bass("TRN2", target_bir_lowering=False)
    dt = nc.dram_tensor
    ikind = "ExternalOutput" if debug else "Internal"

    x_own = dt("x_own", [T, D], F32, kind="ExternalInput").ap()
    x_prev = dt("x_prev", [T, D], F32, kind="ExternalInput").ap()
    x_ownT = dt("x_ownT", [D, T], F32, kind="ExternalInput").ap()
    x_prevT = dt("x_prevT", [D, T], F32, kind="ExternalInput").ap()
    qkvT = dt("qkvT", [D, 3 * D], F32, kind="ExternalInput").ap()
    outT = dt("outT", [D, D], F32, kind="ExternalInput").ap()
    gateT = dt("gateT", [D, D], F32, kind="ExternalInput").ap()
    upT = dt("upT", [D, D], F32, kind="ExternalInput").ap()
    downT = dt("downT", [D, D], F32, kind="ExternalInput").ap()
    expertN = dt("expertN", [D, D], F32, kind="ExternalInput").ap()
    wgT = dt("wgT", [D, 8], F32, kind="ExternalInput").ap()
    norm_wc = dt("norm_wc", [128, DC], F32, kind="ExternalInput").ap()
    dd_tt = dt("dd_t", [H, BS, BS], F32, kind="ExternalInput").ap()
    qd_pairt = dt("qd_pair", [16, 128, BS], F32, kind="ExternalInput").ap()
    kd_pairt = dt("kd_pair", [16, 128, BS], F32, kind="ExternalInput").ap()
    kdp_tokgt = dt("kdp_tokg", [NG, TCH, 128, GH * DH], F32, kind="ExternalInput").ap()
    ident16 = dt("ident16", [16, 16], F32, kind="ExternalInput").ap()
    id128f = dt("id128f", [128, 128], F32, kind="ExternalInput").ap()
    id128b = dt("id128b", [128, 128], BF16, kind="ExternalInput").ap()

    attn_scr = dt("attn_scr", [TCH, D, 128], BF16, kind=ikind).ap()
    h_scr = dt("h_scr", [TCH, D, 128], BF16, kind=ikind).ap()
    wbf = {}
    for nm in ("out", "gate", "up", "down", "exp"):
        wbf[nm] = dt("wbf_" + nm, [DC, 128, D], BF16, kind="Internal").ap()
    rs_scr = dt("rs_scr", [T], F32, kind=ikind).ap()
    rrow_scr = dt("rrow_scr", [2, TCH, 128], F32, kind=ikind).ap()
    out = dt("out", [T, D], F32, kind="ExternalOutput").ap()

    _, _, _, _, bd = _slope_tables()

    with contextlib.ExitStack() as stk:
        sb = lambda name, shape, dtype: stk.enter_context(nc.sbuf_tensor(name, shape, dtype))
        psb = lambda name, shape, dtype: stk.enter_context(nc.psum_tensor(name, shape, dtype))

        actT = sb("actT", [128, DC * T], BF16)
        big0 = sb("big0", [128, 8192], BF16)
        big1 = sb("big1", [128, 8192], BF16)
        big2 = sb("big2", [128, 8192], BF16)
        big3 = sb("big3", [128, 8192], BF16)
        abuf = sb("abuf", [128, 8192], BF16)
        qtg = sb("qtg", [128, 4096], BF16)
        stgf = sb("stgf", [128, 4096], F32)
        ddg = sb("ddg", [128, 2048], F32)
        state_f = sb("state_f", [128, 1024], F32)
        state_b = sb("state_b", [128, 1024], BF16)
        qdp = sb("qdp", [128, 512], F32)
        kdt = sb("kdt", [128, 512], F32)
        kdp_st = sb("kdp_st", [128, 512], F32)
        qpb = sb("qpb", [128, 512], BF16)
        awb = sb("awb", [128, 512], BF16)
        sqt = sb("sqt", [128, 512], BF16)
        msqb = sb("msqb", [1, 2048], F32)
        nw = sb("nw", [128, DC], F32)
        wgstg = sb("wgstg", [128, 128], F32)
        wgbf = sb("wgbf", [128, 128], BF16)
        lnt = sb("lnt", [128, 32], F32)
        rmr = sb("rmr", [128, 32], F32)
        rows32 = sb("rows32", [32, 128], F32)
        onesb = sb("onesb", [128, 1], BF16)
        onesf = sb("onesf", [1, 128], F32)
        epsb = sb("epsb", [128, 1], F32)
        rs16 = sb("rs16", [16, 128], F32)
        id16 = sb("id16", [16, 16], F32)
        id128fb = sb("id128fb", [128, 128], F32)
        id128bb = sb("id128bb", [128, 128], BF16)
        rstd_tok = sb("rstd_tok", [128, DC], F32)
        rho_all = sb("rho_all", [128, TCH], F32)
        rhot = sb("rhot", [128, 32], F32)
        obuf = sb("obuf", [128, 512], F32)
        kp32 = sb("kp32", [128, 256], F32)

        psA = psb("psA", [128, 512], F32)
        psB = psb("psB", [128, 512], F32)
        psC = psb("psC", [128, 512], F32)
        psD = psb("psD", [128, 512], F32)
        aw0 = psb("aw0", [128, 512], F32)
        aw1 = psb("aw1", [128, 512], F32)
        psAT = psb("psAT", [128, 256], F32)
        psS = psb("psS", [128, 256], F32)

        SEMS = ["ld1", "ld2", "ld3", "ps1", "pe", "dv", "ac"]
        sems = {n: stk.enter_context(nc.semaphore(n)) for n in SEMS}

        P = Prog()
        P.sem_init(SEMS)
        A = P.add
        if debug:
            dbg_state = dt("dbg_state", [128, 1024], F32, kind="ExternalOutput").ap()
            dbg_rho = dt("dbg_rho", [128, TCH], F32, kind="ExternalOutput").ap()
            dbg_rstd = dt("dbg_rstd", [128, DC], F32, kind="ExternalOutput").ap()
            dbg_q = dt("dbg_q", [128, 4096], BF16, kind="ExternalOutput").ap()
            dbg_act = dt("dbg_act", [128, DC * T], BF16, kind="ExternalOutput").ap()
            dbg_kt = dt("dbg_kt", [128, 4096], BF16, kind="ExternalOutput").ap()
            dbg_kp = dt("dbg_kp", [128, 4096], BF16, kind="ExternalOutput").ap()
            dbg_v = dt("dbg_v", [128, 4096], BF16, kind="ExternalOutput").ap()

        kpT = big0[:, 0:4096]
        v_tok = big0[:, 4096:8192]
        kp_tok = big1[:, 0:4096]
        cast_bf_stage = big1
        kT_g = big3[:, 0:4096]
        wq_bf = big3[:, 4096:8192]
        wkv_bf = big2
        wv_bf = big2[:, 0:4096]
        wk_bf = big2[:, 4096:8192]
        k_tok = big0[:, 0:4096]
        wgt_bf = big0[:, 0:4096]

        t_nw = A("sp", lambda e: e.dma_start(out=nw[:, :], in_=norm_wc[:, :]), inc=("ld1", 16))
        t_wg = A("sp", lambda e: e.dma_start(out=wgstg[:, :], in_=wgT.rearrange("(dc p) e -> p dc e", p=128)), inc=("ld1", 16))
        t_id = A("sp", lambda e: e.dma_start(out=id16[:, :], in_=ident16[:, :]), inc=("ld1", 16))
        t_idf = A("sp", lambda e: e.dma_start(out=id128fb[:, :], in_=id128f[:, :]), inc=("ld1", 16))
        t_idb = A("sp", lambda e: e.dma_start(out=id128bb[:, :], in_=id128b[:, :]), inc=("ld1", 16))
        t_wgc = A("dve", lambda e: e.tensor_copy(wgbf[:, :], wgstg[:, :]), waits=[("ld1", t_wg)], inc=("dv", 1))
        A("dve", lambda e: e.memset(onesb[:, :], 1.0), inc=("dv", 1))
        A("dve", lambda e: e.memset(onesf[:, :], 1.0), inc=("dv", 1))
        t_eps = A("dve", lambda e: e.memset(epsb[:, :], EPS), inc=("dv", 1))

        slot_cast_tok = [None, None]
        slot_store_tok = [None, None]
        idx = 0
        wlist = [(outT, "out", True), (upT, "up", False),
                 (downT, "down", False), (expertN, "exp", False)]
        for (wsrc, wnm, fold_norm) in wlist:
            for dc in range(DC):
                sl = idx % 2
                fsrc = stgf[:, sl * 2048:(sl + 1) * 2048]
                bdst = cast_bf_stage[:, sl * 2048:(sl + 1) * 2048]
                w_ld = [("dv", slot_cast_tok[sl])] if slot_cast_tok[sl] is not None else []
                tok_l = A("sp", lambda e, o=fsrc, i=wsrc[dc * 128:(dc + 1) * 128, :]:
                          e.dma_start(out=o, in_=i), waits=w_ld, inc=("ld1", 16))
                w_c = [("ld1", tok_l)]
                if slot_store_tok[sl] is not None:
                    w_c.append(("ps1", slot_store_tok[sl]))
                if fold_norm:
                    tok_c = A("dve", lambda e, o=bdst, i=fsrc, s=nw[:, dc:dc + 1]:
                              e.tensor_scalar(o, i, s, None, ALU.mult),
                              waits=w_c + [("ld1", t_nw)], inc=("dv", 1))
                else:
                    tok_c = A("dve", lambda e, o=bdst, i=fsrc: e.tensor_copy(o, i),
                              waits=w_c, inc=("dv", 1))
                slot_cast_tok[sl] = tok_c
                tok_s = A("pool", lambda e, o=wbf[wnm][dc], i=bdst: e.dma_start(out=o, in_=i),
                          waits=[("dv", tok_c)], inc=("ps1", 16))
                slot_store_tok[sl] = tok_s
                idx += 1
        ph1_ps = P.now("ps1")

        def emit_ln_feature(xsrc_tok, xsrcT):
            dv_start = P.now("dv")
            pe_start = P.now("pe")
            ac_start = P.now("ac")
            ps_start = P.now("ps1")
            chain_tok = [None, None]
            for tch in range(TCH):
                sl = tch % 2
                xf = stgf[:, sl * 2048:(sl + 1) * 2048]
                cols = lnt[:, sl * 16:sl * 16 + 16]
                if chain_tok[sl] is not None:
                    w_ld = [("dv", chain_tok[sl])]
                else:
                    w_ld = []
                    if dv_start:
                        w_ld.append(("dv", dv_start))
                    if pe_start:
                        w_ld.append(("pe", pe_start))
                tok_l = A("sp", lambda e, o=xf, i=xsrc_tok[tch * 128:(tch + 1) * 128, :]:
                          e.dma_start(out=o, in_=i), waits=w_ld, inc=("ld1", 16))
                t_s1 = A("dve", lambda e, o=cols[:, 0:1], i=xf: e.tensor_reduce(o, i, AX, ALU.add),
                         waits=[("ld1", tok_l)], inc=("dv", 1))
                t_sq = A("act", lambda e, o=actT[:, 0:2048], i=xf, a=cols[:, 1:2]:
                         e.activation(o, i, AF.Square, accum_out=a),
                         waits=[("ld1", tok_l)], inc=("ac", 1))
                t_m = A("dve", lambda e, o=cols[:, 2:3], i=cols[:, 0:1]:
                        e.tensor_scalar(o, i, 1.0 / D, None, ALU.mult),
                        waits=[("dv", t_s1)], inc=("dv", 1))
                t_ms = A("dve", lambda e, o=cols[:, 3:4], i=cols[:, 1:2]:
                         e.tensor_scalar(o, i, 1.0 / D, None, ALU.mult),
                         waits=[("ac", t_sq), ("dv", t_m)], inc=("dv", 1))
                t_mm = A("dve", lambda e, o=cols[:, 4:5], i=cols[:, 2:3], j=cols[:, 2:3]:
                         e.tensor_tensor(o, i, j, ALU.mult), waits=[("dv", t_ms)], inc=("dv", 1))
                t_var = A("dve", lambda e, o=cols[:, 5:6], i=cols[:, 3:4], j=cols[:, 4:5]:
                          e.tensor_tensor(o, i, j, ALU.subtract), waits=[("dv", t_mm)], inc=("dv", 1))
                t_ln = A("act", lambda e, o=cols[:, 6:7], i=cols[:, 5:6]:
                         e.activation(o, i, AF.Ln, bias=epsb[:, :]),
                         waits=[("dv", t_var), ("dv", t_eps)], inc=("ac", 1))
                t_r = A("act", lambda e, o=rmr[:, tch:tch + 1], i=cols[:, 6:7]:
                        e.activation(o, i, AF.Exp, scale=-0.5), waits=[("ac", t_ln)], inc=("ac", 1))
                t_mr = A("dve", lambda e, o=cols[:, 8:9], i=cols[:, 2:3], j=rmr[:, tch:tch + 1]:
                         e.tensor_tensor(o, i, j, ALU.mult), waits=[("ac", t_r)], inc=("dv", 1))
                t_nmr = A("dve", lambda e, o=rmr[:, 16 + tch:17 + tch], i=cols[:, 8:9]:
                          e.tensor_scalar(o, i, -1.0, None, ALU.mult), waits=[("dv", t_mr)], inc=("dv", 1))
                chain_tok[sl] = t_nmr
            t_tp = A("pe", lambda e: e.transpose(psS[0:32, 128:256], rmr[:, :], id128fb[:, :]),
                     waits=[("dv", chain_tok[0]), ("dv", chain_tok[1]), ("ld1", t_idf),
                            ("dv", P.now("dv"))], inc=("pe", 1))
            t_rows = A("dve", lambda e: e.tensor_copy(rows32[:, :], psS[0:32, 128:256]),
                       waits=[("pe", t_tp)], inc=("dv", 1))
            t_w = A("pool", lambda e: e.dma_start(out=rrow_scr.rearrange("k t p -> (k t) p"), in_=rows32[:, :]),
                    waits=[("dv", t_rows), ("ps1", ps_start)], inc=("ps1", 16))
            apply_last = None
            pe_bcast_last = None
            ld_last = [None, None]
            for cp in range(2):
                w_rb = [("ps1", t_w)]
                if pe_bcast_last is not None:
                    w_rb.append(("pe", pe_bcast_last))
                t_rb = A("pool", lambda e, o=msqb[0:1, 0:1024],
                         i=rrow_scr[0:1, 8 * cp:8 * cp + 8, :].rearrange("k t p -> k (t p)"):
                         e.dma_start(out=o, in_=i), waits=w_rb, inc=("ps1", 16))
                t_nb = A("pool", lambda e, o=msqb[0:1, 1024:2048],
                         i=rrow_scr[1:2, 8 * cp:8 * cp + 8, :].rearrange("k t p -> k (t p)"):
                         e.dma_start(out=o, in_=i), inc=("ps1", 16))
                ev_toks = []
                for bi, (ps, src0) in enumerate([(psA, 0), (psB, 1024), (psC, 512), (psD, 1536)]):
                    t_mm_last = None
                    for i4 in range(4):
                        w0 = ()
                        if bi == 0 and i4 == 0:
                            w0 = [("ps1", t_nb)]
                            if apply_last is not None:
                                w0.append(("dv", apply_last))
                            else:
                                w0.append(("dv", P.now("dv")))
                                if ac_start:
                                    w0.append(("ac", ac_start))
                        t_mm_last = A("pe", lambda e, o=ps, src=msqb[0:1, src0 + i4 * 128:src0 + (i4 + 1) * 128],
                                      off_=i4 * 128:
                                      e.matmul(o[0:128, off_:off_ + 128], onesf[:, :], src, start=True, stop=True),
                                      waits=w0, inc=(("pe", 1) if i4 == 3 else None))
                    dst0 = [0, 512, 1024, 1536][bi]
                    t_ev = A("dve", lambda e, o=ddg[:, dst0:dst0 + 512], i=ps[0:128, 0:512]:
                             e.tensor_copy(o, i), waits=[("pe", t_mm_last)], inc=("dv", 1))
                    ev_toks.append(t_ev)
                    pe_bcast_last = t_mm_last
                t_bc = ev_toks[-1]
                for dc in range(DC):
                    sl = dc % 2
                    xf = stgf[:, sl * 2048:(sl + 1) * 2048]
                    if ld_last[sl] is not None:
                        w_l = [("dv", ld_last[sl])]
                    else:
                        w_l = [("dv", t_bc)]
                    tok_l = A("sp", lambda e, o=xf[:, 0:1024],
                              i=xsrcT[dc * 128:(dc + 1) * 128, cp * 1024:cp * 1024 + 1024]:
                              e.dma_start(out=o, in_=i), waits=w_l, inc=("ld1", 16))
                    last_t2 = None
                    for c2 in range(2):
                        t1 = A("dve", lambda e, o=sqt[:, :], i=xf[:, c2 * 512:(c2 + 1) * 512],
                               j=ddg[:, c2 * 1024:c2 * 1024 + 512]:
                               e.tensor_tensor(o, i, j, ALU.mult),
                               waits=[("ld1", tok_l), ("dv", t_bc)], inc=("dv", 1))
                        t2 = A("dve", lambda e, o=actT[:, dc * T + cp * 1024 + c2 * 512:dc * T + cp * 1024 + c2 * 512 + 512],
                               i=sqt[:, :], j=ddg[:, c2 * 1024 + 512:c2 * 1024 + 1024]:
                               e.tensor_tensor(o, i, j, ALU.add), waits=[("dv", t1)], inc=("dv", 1))
                        last_t2 = t2
                    ld_last[sl] = last_t2
                    apply_last = last_t2
            return apply_last

        actT_ready = emit_ln_feature(x_prev, x_prevT)

        prev_state_dv = None
        for g in range(NG):
            pe0 = P.now("pe"); ac0 = P.now("ac"); dv0 = P.now("dv")
            cast_toks = []
            for dc in range(DC):
                sl = dc % 2
                fsrc = stgf[:, sl * 2048:sl * 2048 + 512]
                w_ld = [("pe", pe0)] if pe0 else []
                if slot_cast_tok[sl] is not None:
                    w_ld.append(("dv", slot_cast_tok[sl]))
                tok_l = A("sp", lambda e, o=fsrc, i=qkvT[dc * 128:(dc + 1) * 128, g * 768 + 256:g * 768 + 768]:
                          e.dma_start(out=o, in_=i), waits=w_ld, inc=("ld1", 16))
                tok_c = A("dve", lambda e, o=wkv_bf[:, dc * 512:(dc + 1) * 512], i=fsrc:
                          e.tensor_copy(o, i), waits=[("ld1", tok_l)], inc=("dv", 1))
                slot_cast_tok[sl] = tok_c
                cast_toks.append(tok_c)
            evac_toks = []
            for tch in range(TCH):
                ps = psA if tch % 2 == 0 else psB
                w0 = [("dv", cast_toks[-1]), ("dv", actT_ready), ("ac", ac0)]
                if len(evac_toks) >= 3:
                    w0.append(("ac", evac_toks[-3]))
                tok = None
                for dc in range(DC):
                    tok = A("pe", lambda e, o=ps, l=actT[:, dc * T + tch * 128:dc * T + tch * 128 + 128],
                            r=wkv_bf[:, dc * 512:(dc + 1) * 512], s=(dc == 0), p=(dc == DC - 1):
                            e.matmul(o[0:128, 0:512], l, r, start=s, stop=p),
                            waits=(w0 if dc == 0 else ()), inc=(("pe", 1) if dc == DC - 1 else None))
                tka = A("act", lambda e, o=k_tok[:, tch * 256:(tch + 1) * 256], i=ps[0:128, 0:256]:
                        e.activation(o, i, AF.Silu), waits=[("pe", tok)], inc=("ac", 1))
                tkv = A("act", lambda e, o=v_tok[:, tch * 256:(tch + 1) * 256], i=ps[0:128, 256:512]:
                        e.activation(o, i, AF.Silu), waits=[("pe", tok)], inc=("ac", 1))
                evac_toks.extend([tka, tkv])
            kp_toks = []
            for tch in range(TCH):
                sl = tch % 2
                w_l = [("dv", dv0)] if dv0 else []
                if len(kp_toks) >= 2:
                    w_l = [("dv", kp_toks[-2])]
                tok_l = A("sp", lambda e, o=kdp_st[:, sl * 256:(sl + 1) * 256], i=kdp_tokgt[g, tch]:
                          e.dma_start(out=o, in_=i), waits=w_l, inc=("ld1", 16))
                tok_m = A("dve", lambda e, o=kp_tok[:, tch * 256:(tch + 1) * 256],
                          i=k_tok[:, tch * 256:(tch + 1) * 256], j=kdp_st[:, sl * 256:(sl + 1) * 256]:
                          e.tensor_tensor(o, i, j, ALU.mult),
                          waits=[("ld1", tok_l), ("ac", evac_toks[2 * tch])], inc=("dv", 1))
                kp_toks.append(tok_m)
            state_evac = []
            for hl in range(GH):
                h = 4 * g + hl
                half = 64 * (h % 2)
                colb = 64 * (h // 2)
                ph = 64 * (hl % 2)
                w0 = [("dv", kp_toks[-1]), ("ac", evac_toks[-1])]
                if len(state_evac) >= 2:
                    w0.append(("dv", state_evac[-2]))
                tok = None
                for tch in range(TCH):
                    tok = A("pe", lambda e, o=psS, l=kp_tok[:, tch * 256 + hl * 64:tch * 256 + hl * 64 + 64],
                            r=v_tok[:, tch * 256 + hl * 64:tch * 256 + hl * 64 + 64],
                            s=(tch == 0), p=(tch == TCH - 1), ph_=ph:
                            e.matmul(o[ph_:ph_ + 64, 0:64], l, r, start=s, stop=p),
                            waits=(w0 if tch == 0 else ()), inc=(("pe", 1) if tch == TCH - 1 else None))
                tok_e = A("dve", lambda e, o=state_f[half:half + 64, colb:colb + 64], i=psS[ph:ph + 64, 0:64]:
                          e.tensor_copy(o, i), waits=[("pe", tok)], inc=("dv", 1))
                state_evac.append(tok_e)
            prev_state_dv = state_evac[-1]
        t_stb0 = A("dve", lambda e: e.tensor_copy(state_b[:, :], state_f[:, :]),
                   waits=[("dv", prev_state_dv)], inc=("dv", 1))

        actT_ready = emit_ln_feature(x_own, x_ownT)
        t_msq0 = A("dve", lambda e: e.memset(msqb[:, :], 0.0), waits=[("dv", actT_ready)], inc=("dv", 1))

        attn_set_store = [None, None]
        last_msq_dv = t_msq0
        aw_bank_user = [None, None]
        attn_ps_user = [None, None]
        delta_user = [None, None]
        psT_user = [None, None]
        state_bf_tok = {}
        for g in range(NG):
            pe0 = P.now("pe"); ac0 = P.now("ac"); dv0 = P.now("dv"); ps0 = P.now("ps1")
            aset = g % 2
            t_dd = []
            for hlh in range(GH):
                tok = A("sp", lambda e, o=ddg[:, hlh * 512:(hlh + 1) * 512].rearrange("p (jc i) -> p jc i", jc=2),
                        i=dd_tt[4 * g + hlh].rearrange("(jc p) i -> p jc i", p=128):
                        e.dma_start(out=o, in_=i),
                        waits=([("dv", dv0)] if hlh == 0 else []), inc=("ld2", 16))
                t_dd.append(tok)
            t_qd = A("sp", lambda e, o=qdp.rearrange("p (pr i) -> p pr i", pr=2),
                     i=qd_pairt[2 * g:2 * g + 2].rearrange("pr p i -> p pr i"):
                     e.dma_start(out=o, in_=i), waits=([("dv", dv0)] if g > 0 else []), inc=("ld2", 16))
            t_kd = A("sp", lambda e, o=kdt.rearrange("p (pr i) -> p pr i", pr=2),
                     i=kd_pairt[2 * g:2 * g + 2].rearrange("pr p i -> p pr i"):
                     e.dma_start(out=o, in_=i), waits=([("dv", dv0)] if g > 0 else []), inc=("ld2", 16))
            cast_all = []
            for dc in range(DC):
                sl = dc % 2
                fsrc = stgf[:, sl * 2048:sl * 2048 + 512]
                fsrc2 = stgf[:, sl * 2048 + 512:sl * 2048 + 768]
                fsrc3 = stgf[:, sl * 2048 + 768:sl * 2048 + 1024]
                w_ld = [("pe", pe0)] if pe0 else []
                if slot_cast_tok[sl] is not None:
                    w_ld.append(("dv", slot_cast_tok[sl]))
                tok_l = A("sp", lambda e, o=fsrc, i=qkvT[dc * 128:(dc + 1) * 128, g * 768 + 256:g * 768 + 768]:
                          e.dma_start(out=o, in_=i), waits=w_ld, inc=("ld2", 16))
                tok_l2 = A("sp", lambda e, o=fsrc2, i=qkvT[dc * 128:(dc + 1) * 128, g * 768:g * 768 + 256]:
                           e.dma_start(out=o, in_=i), inc=("ld2", 16))
                tok_l3 = A("sp", lambda e, o=fsrc3, i=gateT[dc * 128:(dc + 1) * 128, g * 256:(g + 1) * 256]:
                           e.dma_start(out=o, in_=i), inc=("ld2", 16))
                tok_ck = A("dve", lambda e, o=wk_bf[:, dc * 256:(dc + 1) * 256], i=fsrc[:, 0:256]:
                           e.tensor_copy(o, i), waits=[("ld2", tok_l)], inc=("dv", 1))
                tok_cv = A("dve", lambda e, o=wv_bf[:, dc * 256:(dc + 1) * 256], i=fsrc[:, 256:512]:
                           e.tensor_copy(o, i), inc=("dv", 1))
                tok_cg = A("dve", lambda e, o=wgt_bf[:, dc * 256:(dc + 1) * 256], i=fsrc3:
                           e.tensor_copy(o, i), waits=[("ld2", tok_l3)], inc=("dv", 1))
                tok_c2 = A("dve", lambda e, o=wq_bf[:, dc * 256:(dc + 1) * 256], i=fsrc2:
                           e.tensor_copy(o, i), waits=[("ld2", tok_l2)], inc=("dv", 1))
                slot_cast_tok[sl] = tok_c2
                cast_all.append(tok_c2)
            q_evac = {}
            kT_evac = {}
            qk_idx = 0
            for which in ("q", "k"):
                wsrc = wq_bf if which == "q" else wk_bf
                dstb = qtg if which == "q" else kT_g
                for pc in range(2):
                    for tm in range(4):
                        ps = psA if qk_idx % 2 == 0 else psB
                        if qk_idx == 0:
                            w0 = [("dv", P.now("dv")), ("ac", ac0), ("dv", actT_ready)]
                        elif qk_idx == 1:
                            w0 = [("ac", ac0)]
                        else:
                            w0 = [("ac", P.now("ac") - 2)]
                        tok = None
                        for dc in range(DC):
                            tok = A("pe", lambda e, o=ps, l=wsrc[:, dc * 256 + pc * 128:dc * 256 + pc * 128 + 128],
                                    r=actT[:, dc * T + tm * 512:dc * T + tm * 512 + 512],
                                    s=(dc == 0), p=(dc == DC - 1):
                                    e.matmul(o[0:128, 0:512], l, r, start=s, stop=p),
                                    waits=(w0 if dc == 0 else ()), inc=(("pe", 1) if dc == DC - 1 else None))
                        te = A("act", lambda e, o=dstb[:, pc * 2048 + tm * 512:pc * 2048 + tm * 512 + 512],
                               i=ps[0:128, 0:512]:
                               e.activation(o, i, AF.Silu), waits=[("pe", tok)], inc=("ac", 1))
                        if which == "q":
                            q_evac[(pc, tm)] = te
                        else:
                            kT_evac[(pc, tm)] = te
                        qk_idx += 1
            evac_toks = []
            last_qk_evac = kT_evac[(1, 3)]
            for tch in range(TCH):
                ps = psA if tch % 2 == 0 else psB
                w0 = [("dv", cast_all[-1])]
                if tch < 2:
                    w0.append(("ac", last_qk_evac))
                if len(evac_toks) >= 2:
                    w0.append(("ac", evac_toks[-2]))
                tok = None
                for dc in range(DC):
                    tok = A("pe", lambda e, o=ps, l=actT[:, dc * T + tch * 128:dc * T + tch * 128 + 128],
                            r=wv_bf[:, dc * 256:(dc + 1) * 256], s=(dc == 0), p=(dc == DC - 1):
                            e.matmul(o[0:128, 0:256], l, r, start=s, stop=p),
                            waits=(w0 if dc == 0 else ()), inc=(("pe", 1) if dc == DC - 1 else None))
                tkv = A("act", lambda e, o=v_tok[:, tch * 256:(tch + 1) * 256], i=ps[0:128, 0:256]:
                        e.activation(o, i, AF.Silu), waits=[("pe", tok)], inc=("ac", 1))
                evac_toks.append(tkv)
            kp_done = None
            tp_last = None
            for tch in range(TCH):
                for pr in range(2):
                    alt = pr
                    w_kp = [("ac", kT_evac[(pr, tch // 4)]), ("ld2", t_kd)]
                    if psT_user[alt] is not None:
                        w_kp.append(("pe", psT_user[alt][0]))
                    t_kp = A("dve", lambda e, o=kp32[:, alt * 128:alt * 128 + 128],
                             i=kT_g[:, pr * 2048 + tch * 128:pr * 2048 + tch * 128 + 128],
                             j=kdt[:, pr * 256 + (tch % 2) * 128:pr * 256 + (tch % 2) * 128 + 128]:
                             e.tensor_tensor(o, i, j, ALU.mult), waits=w_kp, inc=("dv", 1))
                    w_tp = [("dv", t_kp), ("ld1", t_idf)]
                    if psT_user[alt] is not None:
                        w_tp.append(("dv", psT_user[alt][1]))
                    if tch == 0 and pr == 0:
                        w_tp.append(("dv", last_msq_dv))
                    tok_t = A("pe", lambda e, i=kp32[:, alt * 128:alt * 128 + 128], a=alt:
                              e.transpose(psS[0:128, a * 128:a * 128 + 128], i, id128fb[:, :]),
                              waits=w_tp, inc=("pe", 1))
                    tok_e = A("dve", lambda e, o=kp_tok[:, tch * 256 + pr * 128:tch * 256 + pr * 128 + 128],
                              i=psS[0:128, alt * 128:alt * 128 + 128]:
                              e.tensor_copy(o, i), waits=[("pe", tok_t)], inc=("dv", 1))
                    psT_user[alt] = (tok_t, tok_e)
                    kp_done = tok_e
            inter_last = {0: None, 1: None}
            for m in range(NB):
                qp_tok = {}
                for pr in range(2):
                    w_q = [("ac", q_evac[(pr, m // 2)]), ("ld2", t_qd)]
                    if inter_last[pr] is not None:
                        w_q.append(("pe", inter_last[pr]))
                    qp_tok[pr] = A("dve", lambda e, o=qpb[:, pr * 256:(pr + 1) * 256],
                                   i=qtg[:, pr * 2048 + m * 256:pr * 2048 + m * 256 + 256],
                                   j=qdp[:, pr * 256:(pr + 1) * 256]:
                                   e.tensor_tensor(o, i, j, ALU.mult), waits=w_q, inc=("dv", 1))
                for hl in range(GH):
                    pr = hl // 2
                    hh = hl % 2
                    h = 4 * g + hl
                    gp = 2 * g + pr
                    aw_pe = []
                    for jc in range(2):
                        psaw = aw0 if jc == 0 else aw1
                        w0 = [("ac", kT_evac[(pr, m // 2)]), ("ac", q_evac[(pr, m // 2)])]
                        if aw_bank_user[jc] is not None:
                            w0.append(("dv", aw_bank_user[jc]))
                        tok = A("pe", lambda e, o=psaw,
                                l=kT_g[:, pr * 2048 + m * 256 + jc * 128:pr * 2048 + m * 256 + jc * 128 + 128],
                                r=qtg[:, pr * 2048 + m * 256:pr * 2048 + m * 256 + 256], b=hh * 64:
                                e.matmul(o[0:128, 0:256], l[b:b + 64, :], r[b:b + 64, :], start=True, stop=True),
                                waits=w0, inc=("pe", 1))
                        aw_pe.append(tok)
                    dd_tok = []
                    for jc in range(2):
                        psaw = aw0 if jc == 0 else aw1
                        tok = A("dve", lambda e, o=awb[:, jc * 256:(jc + 1) * 256], i=psaw[0:128, 0:256],
                                j=ddg[:, (hl * 2 + jc) * 256:(hl * 2 + jc) * 256 + 256]:
                                e.tensor_tensor(o, i, j, ALU.mult),
                                waits=[("pe", aw_pe[jc]), ("ld2", t_dd[hl])], inc=("dv", 1))
                        dd_tok.append(tok)
                        aw_bank_user[jc] = tok
                    alt = (m * 4 + hl) % 2
                    pb = 64 * alt
                    w0 = [("dv", dd_tok[1]), ("dv", kp_done), ("dv", qp_tok[pr]),
                          ("ac", evac_toks[-1]),
                          ("dv", state_bf_tok.get((hh, gp), t_stb0))]
                    if attn_ps_user[alt] is not None:
                        w0.append(("ac", attn_ps_user[alt]))
                    A("pe", lambda e, o=psAT, l=v_tok[:, (2 * m) * 256 + hl * 64:(2 * m) * 256 + hl * 64 + 64],
                      r=awb[:, 0:256], pb_=pb:
                      e.matmul(o[pb_:pb_ + 64, 0:256], l, r, start=True, stop=False), waits=w0)
                    A("pe", lambda e, o=psAT, l=v_tok[:, (2 * m + 1) * 256 + hl * 64:(2 * m + 1) * 256 + hl * 64 + 64],
                      r=awb[:, 256:512], pb_=pb:
                      e.matmul(o[pb_:pb_ + 64, 0:256], l, r, start=False, stop=False))
                    tok_inter = A("pe", lambda e, o=psAT, l=state_b[hh * 64:hh * 64 + 64, gp * 64:gp * 64 + 64],
                                  r=qpb[:, pr * 256:pr * 256 + 256], pb_=pb, b=hh * 64:
                                  e.matmul(o[pb_:pb_ + 64, 0:256], l, r[b:b + 64, :], start=False, stop=True),
                                  inc=("pe", 1))
                    inter_last[pr] = tok_inter
                    alt2 = (m * 4 + hl) % 2
                    pb2 = 64 * alt2
                    w0 = [("ac", evac_toks[-1]), ("dv", kp_done)]
                    if delta_user[alt2] is not None:
                        w0.append(("dv", delta_user[alt2]))
                    if m == 0 and hl == 0:
                        w0.append(("dv", last_msq_dv))
                    A("pe", lambda e, o=psS, l=kp_tok[:, (2 * m) * 256 + hl * 64:(2 * m) * 256 + hl * 64 + 64],
                      r=v_tok[:, (2 * m) * 256 + hl * 64:(2 * m) * 256 + hl * 64 + 64], pb_=pb2:
                      e.matmul(o[pb_:pb_ + 64, 64:128], l, r, start=True, stop=False), waits=w0)
                    tok_dl = A("pe", lambda e, o=psS, l=kp_tok[:, (2 * m + 1) * 256 + hl * 64:(2 * m + 1) * 256 + hl * 64 + 64],
                               r=v_tok[:, (2 * m + 1) * 256 + hl * 64:(2 * m + 1) * 256 + hl * 64 + 64], pb_=pb2:
                               e.matmul(o[pb_:pb_ + 64, 64:128], l, r, start=False, stop=True), inc=("pe", 1))
                    t_up = A("dve", lambda e, o=state_f[hh * 64:hh * 64 + 64, gp * 64:gp * 64 + 64],
                             i=psS[pb2:pb2 + 64, 64:128], bd_=float(bd[h]):
                             e.scalar_tensor_tensor(o, o, bd_, i, ALU.mult, ALU.add),
                             waits=[("pe", tok_dl)], inc=("dv", 1))
                    delta_user[alt2] = t_up
                    t_bf = A("dve", lambda e, o=state_b[hh * 64:hh * 64 + 64, gp * 64:gp * 64 + 64],
                             i=state_f[hh * 64:hh * 64 + 64, gp * 64:gp * 64 + 64]:
                             e.tensor_copy(o, i), waits=[("dv", t_up), ("pe", tok_inter)], inc=("dv", 1))
                    state_bf_tok[(hh, gp)] = t_bf
                    w0 = [("pe", tok_inter)]
                    if m == 0 and hl == 0 and attn_set_store[aset] is not None:
                        w0.append(("ps1", attn_set_store[aset]))
                    tok_ev = A("act", lambda e, o=abuf[:, (aset * 2 + pr) * 2048 + m * 256:(aset * 2 + pr) * 2048 + m * 256 + 256],
                               i=psAT[pb:pb + 64, 0:256], b=hh * 64:
                               e.activation(o[b:b + 64, :], i, AF.Identity), waits=w0, inc=("ac", 1))
                    attn_ps_user[alt] = tok_ev
            tail_dv0 = P.now("dv")
            prev_msq_mm = None
            for pr in range(2):
                tile = abuf[:, (aset * 2 + pr) * 2048:(aset * 2 + pr) * 2048 + 2048]
                for c5 in range(4):
                    w_sq = [("ac", P.now("ac"))] if (pr == 0 and c5 == 0) else []
                    if prev_msq_mm is not None:
                        w_sq.append(("pe", prev_msq_mm))
                    t_sq = A("act", lambda e, o=sqt[:, :], i=tile[:, c5 * 512:(c5 + 1) * 512]:
                             e.activation(o, i, AF.Square), waits=w_sq, inc=("ac", 1))
                    for h5 in range(2):
                        w0 = [("ac", t_sq), ("dv", last_msq_dv), ("dv", tail_dv0)]
                        tok_mm = A("pe", lambda e, o=psS, i=sqt[:, h5 * 256:(h5 + 1) * 256]:
                                   e.matmul(o[0:1, 0:256], onesb[:, :], i, start=True, stop=True),
                                   waits=w0, inc=("pe", 1))
                        prev_msq_mm = tok_mm
                        last_msq_dv = A("dve", lambda e, o=msqb[0:1, c5 * 512 + h5 * 256:c5 * 512 + h5 * 256 + 256],
                                        i=psS[0:1, 0:256]:
                                        e.tensor_tensor(o, o, i, ALU.add),
                                        waits=[("pe", tok_mm), ("dv", last_msq_dv)], inc=("dv", 1))
            gm_ev = None
            last_gm_dv = None
            for pr in range(2):
                tile = abuf[:, (aset * 2 + pr) * 2048:(aset * 2 + pr) * 2048 + 2048]
                mul_toks = []
                for tm in range(4):
                    ps = psA if tm % 2 == 0 else psB
                    w0 = [("dv", tok_cg), ("pe", prev_msq_mm), ("dv", last_msq_dv)]
                    if gm_ev is not None:
                        w0.append(("ac", gm_ev))
                    tok = None
                    for dc in range(DC):
                        tok = A("pe", lambda e, o=ps, l=wgt_bf[:, dc * 256 + pr * 128:dc * 256 + pr * 128 + 128],
                                r=actT[:, dc * T + tm * 512:dc * T + tm * 512 + 512],
                                s=(dc == 0), p=(dc == DC - 1):
                                e.matmul(o[0:128, 0:512], l, r, start=s, stop=p),
                                waits=(w0 if dc == 0 else ()), inc=(("pe", 1) if dc == DC - 1 else None))
                    w_sg = [("pe", tok)]
                    if mul_toks:
                        w_sg.append(("dv", mul_toks[-1]))
                    elif last_gm_dv is not None:
                        w_sg.append(("dv", last_gm_dv))
                    t_sg = A("act", lambda e, o=sqt[:, :], i=ps[0:128, 0:512]:
                             e.activation(o, i, AF.Sigmoid), waits=w_sg, inc=("ac", 1))
                    gm_ev = t_sg
                    t_gm = A("dve", lambda e, o=tile[:, tm * 512:(tm + 1) * 512],
                             i=tile[:, tm * 512:(tm + 1) * 512], j=sqt[:, :]:
                             e.tensor_tensor(o, i, j, ALU.mult), waits=[("ac", t_sg)], inc=("dv", 1))
                    mul_toks.append(t_gm)
                tok_st = A("pool", lambda e,
                           o=attn_scr[:, (2 * g + pr) * 128:(2 * g + pr) * 128 + 128, :].rearrange("tc j t -> j tc t"),
                           i=tile.rearrange("p (tc t) -> p tc t", tc=TCH):
                           e.dma_start(out=o, in_=i), waits=[("dv", mul_toks[-1])], inc=("ps1", 16))
                attn_set_store[aset] = tok_st
                last_gm_dv = mul_toks[-1]
        if debug:
            A("pool", lambda e: e.dma_start(out=dbg_kt[:, :], in_=kT_g[:, :]),
              waits=[("ac", P.now("ac"))], inc=("ps1", 16))
            A("pool", lambda e: e.dma_start(out=dbg_kp[:, :], in_=kp_tok[:, :]),
              waits=[("dv", P.now("dv"))], inc=("ps1", 16))
            A("pool", lambda e: e.dma_start(out=dbg_v[:, :], in_=v_tok[:, :]),
              waits=[("ac", P.now("ac"))], inc=("ps1", 16))
            A("pool", lambda e: e.dma_start(out=dbg_q[:, :], in_=qtg[:, :]),
              waits=[("ac", P.now("ac"))], inc=("ps1", 16))
            A("pool", lambda e: e.dma_start(out=dbg_act[:, :], in_=actT[:, :]),
              waits=[("dv", P.now("dv"))], inc=("ps1", 16))
        t_var = A("dve", lambda e: e.tensor_scalar(msqb[0:1, 0:2048], msqb[0:1, 0:2048], 1.0 / D, None, ALU.mult),
                  waits=[("dv", last_msq_dv)], inc=("dv", 1))
        t_l = A("act", lambda e: e.activation(msqb[0:1, 0:2048], msqb[0:1, 0:2048], AF.Ln, bias=epsb[0:1, :]),
                waits=[("dv", t_var)], inc=("ac", 1))
        t_e = A("act", lambda e: e.activation(msqb[0:1, 0:2048], msqb[0:1, 0:2048], AF.Exp, scale=-0.5),
                waits=[("ac", t_l)], inc=("ac", 1))
        t_rs = A("pool", lambda e: e.dma_start(out=rs_scr[:], in_=msqb[0:1, 0:2048]),
                 waits=[("ac", t_e)], inc=("ps1", 16))
        t_r16 = A("pool", lambda e: e.dma_start(out=rs16[:, :], in_=rs_scr.rearrange("(c p) -> c p", p=128)),
                  waits=[("ps1", t_rs)], inc=("ps1", 16))
        t_tp = A("pe", lambda e: e.transpose(psS[0:128, 64:80], rs16[:, :], id16[:, :]),
                 waits=[("ps1", t_r16), ("ld1", t_id), ("dv", P.now("dv"))], inc=("pe", 1))
        t_rtok = A("dve", lambda e: e.tensor_copy(rstd_tok[:, :], psS[0:128, 64:80]),
                   waits=[("pe", t_tp)], inc=("dv", 1))

        wup = abuf[:, 4096:6144]
        hb = [abuf[:, 6144:6656], abuf[:, 6656:7168]]
        atc = abuf[:, 0:2048]
        htc = abuf[:, 2048:4096]
        ph6_pe0 = P.now("pe"); ph6_ac0 = P.now("ac"); ph6_ps0 = P.now("ps1")
        wup_user = None
        h_store = [None, None]
        for uc in range(DC):
            w_l = [("ps1", ph1_ps)]
            if uc == 0:
                w_l += [("ps1", ph6_ps0), ("ac", ph6_ac0), ("pe", ph6_pe0)]
            if wup_user is not None:
                w_l.append(("pe", wup_user))
            tok_l = None
            for q4 in range(4):
                tok_l = A("sp", lambda e, o=wup[:, q4 * 512:(q4 + 1) * 512].rearrange("p (dc t) -> p dc t", dc=4),
                          i=wbf["up"][q4 * 4:(q4 + 1) * 4, :, uc * 128:uc * 128 + 128].rearrange("dc p t -> p dc t"):
                          e.dma_start(out=o, in_=i), waits=(w_l if q4 == 0 else ()), inc=("ld3", 16))
            for tm in range(4):
                ps = psA if tm % 2 == 0 else psB
                sl = tm % 2
                w0 = [("ld3", tok_l), ("ac", ph6_ac0)]
                tok = None
                for dc in range(DC):
                    tok = A("pe", lambda e, o=ps, l=wup[:, dc * 128:(dc + 1) * 128],
                            r=actT[:, dc * T + tm * 512:dc * T + tm * 512 + 512], s=(dc == 0), p=(dc == DC - 1):
                            e.matmul(o[0:128, 0:512], l, r, start=s, stop=p),
                            waits=(w0 if dc == 0 else ()), inc=(("pe", 1) if dc == DC - 1 else None))
                wup_user = tok
                w_ev = [("pe", tok)]
                if h_store[sl] is not None:
                    w_ev.append(("ps1", h_store[sl]))
                t_ev = A("act", lambda e, o=hb[sl], i=ps[0:128, 0:512]:
                         e.activation(o, i, AF.Relu), waits=w_ev, inc=("ac", 1))
                tok_s = A("pool", lambda e,
                          o=h_scr[4 * tm:4 * tm + 4, uc * 128:uc * 128 + 128, :].rearrange("tc u t -> u tc t"),
                          i=hb[sl].rearrange("p (tc t) -> p tc t", tc=4):
                          e.dma_start(out=o, in_=i), waits=[("ac", t_ev)], inc=("ps1", 16))
                h_store[sl] = tok_s
        logit_user = None
        for tch in range(TCH):
            w0 = [("dv", t_wgc), ("dv", t_rtok)]
            if logit_user is not None:
                w0 = [("dv", logit_user)]
            tok = None
            for dc in range(DC):
                tok = A("pe", lambda e, o=psS, l=actT[:, dc * T + tch * 128:dc * T + tch * 128 + 128],
                        r=wgbf[:, dc * 8:(dc + 1) * 8], s=(dc == 0), p=(dc == DC - 1):
                        e.matmul(o[0:128, 0:8], l, r, start=s, stop=p),
                        waits=(w0 if dc == 0 else ()), inc=(("pe", 1) if dc == DC - 1 else None))
            t_m1 = A("dve", lambda e, o=rhot[:, 0:1], i=psS[0:128, 0:8]:
                     e.tensor_reduce(o, i, AX, ALU.max), waits=[("pe", tok)], inc=("dv", 1))
            t_d = A("dve", lambda e, o=rhot[:, 8:16], i=psS[0:128, 0:8], s=rhot[:, 0:1]:
                    e.tensor_scalar(o, i, s, None, ALU.subtract), waits=[("dv", t_m1)], inc=("dv", 1))
            logit_user = t_d
            t_eq = A("dve", lambda e, o=rhot[:, 16:24], i=rhot[:, 8:16]:
                     e.tensor_scalar(o, i, 0.0, -1.0e30, ALU.is_equal, op1=ALU.mult),
                     waits=[("dv", t_d)], inc=("dv", 1))
            t_msk = A("dve", lambda e, o=rhot[:, 8:16], i=rhot[:, 8:16], j=rhot[:, 16:24]:
                      e.tensor_tensor(o, i, j, ALU.add), waits=[("dv", t_eq)], inc=("dv", 1))
            t_m2 = A("dve", lambda e, o=rhot[:, 1:2], i=rhot[:, 8:16]:
                     e.tensor_reduce(o, i, AX, ALU.max), waits=[("dv", t_msk)], inc=("dv", 1))
            t_u = A("act", lambda e, o=rhot[:, 2:3], i=rhot[:, 1:2]:
                    e.activation(o, i, AF.Exp), waits=[("dv", t_m2)], inc=("ac", 1))
            t_up1 = A("dve", lambda e, o=rhot[:, 3:4], i=rhot[:, 2:3]:
                      e.tensor_scalar(o, i, 1.0, None, ALU.add), waits=[("ac", t_u)], inc=("dv", 1))
            t_r0 = A("dve", lambda e, o=rhot[:, 4:5], i=rhot[:, 3:4]:
                     e.reciprocal(o, i), waits=[("dv", t_up1)], inc=("dv", 1))
            t_b = A("dve", lambda e, o=rhot[:, 5:6], i=rhot[:, 2:3], j=rhot[:, 4:5]:
                    e.tensor_tensor(o, i, j, ALU.mult), waits=[("dv", t_r0)], inc=("dv", 1))
            t_rho = A("dve", lambda e, o=rho_all[:, tch:tch + 1], i=rhot[:, 4:5], j=rhot[:, 5:6]:
                      e.tensor_tensor(o, i, j, ALU.add), waits=[("dv", t_b)], inc=("dv", 1))
        rho_done = t_rho

        wsl = {"out": big2, "down": big0, "exp": big1}
        ep_A = ep_D = None
        ep_B = ep_C = None
        atc_user = None
        htc_user = None
        last_out_st = None
        for cs in range(4):
            ph7_pe0 = P.now("pe"); ph7_ac0 = P.now("ac"); ph7_dv0 = P.now("dv"); ph7_ps0 = P.now("ps1")
            wsl_tok = {}
            for mat in ("out", "down", "exp"):
                w_l = [("ps1", ph1_ps), ("pe", ph7_pe0)]
                if cs == 0:
                    w_l += [("ac", ph7_ac0), ("dv", ph7_dv0), ("ps1", ph7_ps0)]
                tok = None
                for q4 in range(4):
                    tok = A("sp", lambda e, o=wsl[mat][:, q4 * 2048:(q4 + 1) * 2048].rearrange("p (dc t) -> p dc t", dc=4),
                            i=wbf[mat][q4 * 4:(q4 + 1) * 4, :, cs * 512:cs * 512 + 512].rearrange("dc p t -> p dc t"):
                            e.dma_start(out=o, in_=i), waits=(w_l if q4 == 0 else ()), inc=("ld3", 16))
                wsl_tok[mat] = tok
            for tc in range(TCH):
                w_a = [("pe", atc_user)] if atc_user is not None else [("ps1", ph7_ps0)]
                tok_a = None
                for q4 in range(4):
                    tok_a = A("pool", lambda e, o=atc[:, q4 * 512:(q4 + 1) * 512].rearrange("p (jc t) -> p jc t", jc=4),
                              i=attn_scr[tc, q4 * 512:(q4 + 1) * 512, :].rearrange("(jc p) t -> p jc t", p=128):
                              e.dma_start(out=o, in_=i), waits=(w_a if q4 == 0 else ()), inc=("ps1", 16))
                w_h = [("pe", htc_user)] if htc_user is not None else []
                tok_h = None
                for q4 in range(4):
                    tok_h = A("pool", lambda e, o=htc[:, q4 * 512:(q4 + 1) * 512].rearrange("p (jc t) -> p jc t", jc=4),
                              i=h_scr[tc, q4 * 512:(q4 + 1) * 512, :].rearrange("(jc p) t -> p jc t", p=128):
                              e.dma_start(out=o, in_=i), waits=(w_h if q4 == 0 else ()), inc=("ps1", 16))
                slx = tc % 2
                xst = stgf[:, 3072 + slx * 512:3072 + slx * 512 + 512]
                w_x = [("dv", ep_D)] if ep_D is not None else []
                tok_x = A("sp", lambda e, o=xst, i=x_own[tc * 128:(tc + 1) * 128, cs * 512:cs * 512 + 512]:
                          e.dma_start(out=o, in_=i), waits=w_x, inc=("ld3", 16))

                wsl_map = {"atc": wsl["out"], "htc": wsl["down"], "exp": wsl["exp"]}

                def mmg(psum, mode, wtok, extra):
                    tok = None
                    for ch in range(DC):
                        if mode == "atc":
                            l = atc[:, ch * 128:(ch + 1) * 128]
                        elif mode == "htc":
                            l = htc[:, ch * 128:(ch + 1) * 128]
                        else:
                            l = actT[:, ch * T + tc * 128:ch * T + tc * 128 + 128]
                        w0 = (list(extra) + [("ld3", wtok)]) if ch == 0 else ()
                        tok = A("pe", lambda e, o=psum, l_=l, rr=wsl_map[mode],
                                s=(ch == 0), p=(ch == DC - 1), ch_=ch:
                                e.matmul(o[0:128, 0:512], l_, rr[:, ch_ * 512:(ch_ + 1) * 512], start=s, stop=p),
                                waits=w0, inc=(("pe", 1) if ch == DC - 1 else None))
                    return tok

                exA = [("ps1", tok_a)] + ([("dv", ep_A)] if ep_A is not None else [])
                tok_o = mmg(psA, "atc", wsl_tok["out"], exA)
                atc_user = tok_o
                exC = [("ps1", tok_h)] + ([("ac", ep_C)] if ep_C is not None else [])
                tok_sh = mmg(psC, "htc", wsl_tok["down"], exC)
                htc_user = tok_sh
                exD = [("dv", ep_D)] if ep_D is not None else []
                tok_ex = mmg(psD, "exp", wsl_tok["exp"], exD)
                she = stgf[:, 0:512]; sgb = stgf[:, 512:1024]; gpb = stgf[:, 1024:1536]
                gsb = stgf[:, 1536:2048]; s1b = stgf[:, 2048:2560]; sg2 = stgf[:, 2560:3072]
                t_she = A("act", lambda e, o=she, i=psC[0:128, 0:512]:
                          e.activation(o, i, AF.Identity), waits=[("pe", tok_sh)], inc=("ac", 1))
                ep_C = t_she
                t_gp = A("dve", lambda e, o=gpb, i=psD[0:128, 0:512], s=rho_all[:, tc:tc + 1], j=she:
                         e.scalar_tensor_tensor(o, i, s, j, ALU.mult, ALU.add),
                         waits=[("pe", tok_ex), ("ac", t_she), ("dv", rho_done)], inc=("dv", 1))
                ep_D = t_gp
                t_gs = A("act", lambda e, o=gsb, i=gpb:
                         e.activation(o, i, AF.Silu), waits=[("dv", t_gp)], inc=("ac", 1))
                t_s1 = A("dve", lambda e, o=s1b, i=psA[0:128, 0:512], s=rstd_tok[:, tc:tc + 1]:
                         e.tensor_scalar(o, i, s, None, ALU.mult),
                         waits=[("pe", tok_o), ("dv", t_rtok)], inc=("dv", 1))
                ep_A = t_s1
                t_sg2 = A("dve", lambda e, o=sg2, i=s1b, j=gsb:
                          e.tensor_tensor(o, i, j, ALU.mult),
                          waits=[("dv", t_s1), ("ac", t_gs)], inc=("dv", 1))
                ob = obuf[:, 0:512]
                w_f = [("dv", t_sg2), ("ld3", tok_x)]
                if last_out_st is not None:
                    w_f.append(("ps1", last_out_st))
                t_fin = A("dve", lambda e, o=ob, i=sg2, j=xst:
                          e.tensor_tensor(o, i, j, ALU.add), waits=w_f, inc=("dv", 1))
                tok_os = A("pool", lambda e, o=out[tc * 128:(tc + 1) * 128, cs * 512:cs * 512 + 512], i=ob:
                           e.dma_start(out=o, in_=i), waits=[("dv", t_fin)], inc=("ps1", 16))
                last_out_st = tok_os

        if debug:
            last_out_st = A("pool", lambda e: e.dma_start(out=dbg_state[:, :], in_=state_f[:, :]),
                            waits=[("ps1", last_out_st)], inc=("ps1", 16))
            last_out_st = A("pool", lambda e: e.dma_start(out=dbg_rho[:, :], in_=rho_all[:, :]), inc=("ps1", 16))
            last_out_st = A("pool", lambda e: e.dma_start(out=dbg_rstd[:, :], in_=rstd_tok[:, :]), inc=("ps1", 16))
        A("pool", lambda e: e.memset(lnt[0:1, 0:1], 0.0), waits=[("ps1", last_out_st)], inc=None)

        def emit(eng_name):
            def go(eng):
                for fn, waits, inc in P.stream[eng_name]:
                    for (s, v) in waits:
                        if v:
                            eng.wait_ge(sems[s], v)
                    inst = fn(eng)
                    if inc is not None:
                        inst.then_inc(sems[inc[0]], inc[1])
            return go

        with nc.Block() as block:
            block.tensor(emit("pe"))
            block.vector(emit("dve"))
            block.scalar(emit("act"))
            block.sync(emit("sp"))
            block.gpsimd(emit("pool"))

    return nc


_cache = {}


def kernel(**inputs):
    x = np.asarray(inputs["L0_attn_hidden_states"], dtype=np.float32)
    qkv_w = np.asarray(inputs["L0_attn_qkv_w"], dtype=np.float32)
    out_w = np.asarray(inputs["L0_attn_out_w"], dtype=np.float32)
    gate_w = np.asarray(inputs["L0_attn_gate_w"], dtype=np.float32)
    norm_w = np.asarray(inputs["L0_attn_norm_w"], dtype=np.float32)
    W_up = np.asarray(inputs["L0_ffn_W_up"], dtype=np.float32)
    W_down = np.asarray(inputs["L0_ffn_W_down"], dtype=np.float32)
    W_gate = np.asarray(inputs["L0_ffn_W_gate"], dtype=np.float32)
    W_expert = np.asarray(inputs["L0_ffn_W_expert"], dtype=np.float32)

    perm = np.empty(3 * D, dtype=np.int64)
    for g in range(NG):
        for sub in range(3):
            for hl in range(GH):
                h = 4 * g + hl
                src = h * 192 + sub * 64
                dst = g * 768 + sub * 256 + hl * 64
                perm[dst:dst + 64] = np.arange(src, src + 64)
    qkvT = np.ascontiguousarray(qkv_w.T[:, perm])
    outT = np.ascontiguousarray(out_w.T)
    gateT = np.ascontiguousarray(gate_w.T)
    upT = np.ascontiguousarray(W_up.T)
    downT = np.ascontiguousarray(W_down.T)
    expertN = np.ascontiguousarray(W_expert)
    wgT = np.ascontiguousarray(W_gate.T)
    norm_wc = np.ascontiguousarray(norm_w.reshape(DC, 128).T)
    dd_t, qd_pair, kd_pair, kdp_tokg, _ = make_tables()
    ident16 = np.eye(16, dtype=np.float32)
    id128f = np.eye(128, dtype=np.float32)
    id128b = np.eye(128, dtype=np.float32).astype(ml_dtypes.bfloat16)
    zeros_prev = np.zeros((T, D), np.float32)
    zeros_prevT = np.zeros((D, T), np.float32)

    if "nc" not in _cache:
        _cache["nc"] = build_program()
    nc = _cache["nc"]

    shared = dict(qkvT=qkvT, outT=outT, gateT=gateT, upT=upT, downT=downT,
                  expertN=expertN, wgT=wgT, norm_wc=norm_wc, dd_t=dd_t,
                  qd_pair=qd_pair, kd_pair=kd_pair, kdp_tokg=kdp_tokg,
                  ident16=ident16, id128f=id128f, id128b=id128b)
    in_maps = []
    for c in range(8):
        b, half = c // 2, c % 2
        m = dict(shared)
        xo = np.ascontiguousarray(x[b, half * T:(half + 1) * T, :])
        m["x_own"] = xo
        m["x_ownT"] = np.ascontiguousarray(xo.T)
        if half == 1:
            xp = np.ascontiguousarray(x[b, 0:T, :])
            m["x_prev"] = xp
            m["x_prevT"] = np.ascontiguousarray(xp.T)
        else:
            m["x_prev"] = zeros_prev
            m["x_prevT"] = zeros_prevT
        in_maps.append(m)

    res = run_bass_kernel_spmd(nc, in_maps, core_ids=list(range(8)))
    full = np.empty((4, 2 * T, D), np.float32)
    for c in range(8):
        b, half = c // 2, c % 2
        full[b, half * T:(half + 1) * T, :] = res.results[c]["out"]
    return full
